# version 04
# baseline (speedup 1.0000x reference)
import contextlib
import numpy as np
import concourse.bass as bass
import concourse.mybir as mybir
from concourse.bass_utils import run_bass_kernel_spmd

F32 = mybir.dt.float32
BF16 = mybir.dt.bfloat16
AF = mybir.ActivationFunctionType
ALU = mybir.AluOpType

B, T, H = 16, 2048, 2048
NCORE = 8
BL = B // NCORE
TC = 256
NCH = T // TC
FW = BL * TC
HB = H // 128
I = 5632
IC = I // 128
W1 = TC + 1
INV_H = 1.0 / H
EPS = 1e-5

SEMS = ("sQW", "sQW2", "sQX", "sPE", "sVE", "sACT", "sOUT", "sCI", "sCV", "sCO")


def build_nc():
    nc = bass.Bass("TRN2", target_bir_lowering=False)
    ES = contextlib.ExitStack()

    x_in = nc.dram_tensor("x_in", [BL, NCH, HB, 128, TC], F32, kind="ExternalInput").ap()
    y_out = nc.dram_tensor("y_out", [BL, NCH, HB, 128, TC], F32, kind="ExternalOutput").ap()
    wk_in = nc.dram_tensor("wk_in", [HB, HB, 128, 128], F32, kind="ExternalInput").ap()
    wv_in = nc.dram_tensor("wv_in", [HB, HB, 128, 128], F32, kind="ExternalInput").ap()
    wr_in = nc.dram_tensor("wr_in", [HB, HB, 128, 128], F32, kind="ExternalInput").ap()
    wo_in = nc.dram_tensor("wo_in", [HB, HB, 128, 128], F32, kind="ExternalInput").ap()
    wg_in = nc.dram_tensor("wg_in", [IC, HB, 128, 128], F32, kind="ExternalInput").ap()
    wu_in = nc.dram_tensor("wu_in", [IC, HB, 128, 128], F32, kind="ExternalInput").ap()
    wd_in = nc.dram_tensor("wd_in", [HB, IC, 128, 128], F32, kind="ExternalInput").ap()
    td_in = nc.dram_tensor("td_in", [128, HB], F32, kind="ExternalInput").ap()
    tf_in = nc.dram_tensor("tf_in", [128, HB], F32, kind="ExternalInput").ap()
    tmk_in = nc.dram_tensor("tmk_in", [128, HB], F32, kind="ExternalInput").ap()
    tmv_in = nc.dram_tensor("tmv_in", [128, HB], F32, kind="ExternalInput").ap()
    tmr_in = nc.dram_tensor("tmr_in", [128, HB], F32, kind="ExternalInput").ap()
    gs_in = nc.dram_tensor("gs_in", [128, IC], F32, kind="ExternalInput").ap()
    us_in = nc.dram_tensor("us_in", [128, IC], F32, kind="ExternalInput").ap()
    ds_in = nc.dram_tensor("ds_in", [128, HB], F32, kind="ExternalInput").ap()
    wk_s = nc.dram_tensor("wk_s", [HB, 128, HB * 128], BF16).ap()
    wv_s = nc.dram_tensor("wv_s", [HB, 128, HB * 128], BF16).ap()
    wr_s = nc.dram_tensor("wr_s", [HB, 128, HB * 128], BF16).ap()
    wo_s = nc.dram_tensor("wo_s", [HB, 128, HB * 128], BF16).ap()
    wg_s = nc.dram_tensor("wg_s", [IC, 128, HB * 128], BF16).ap()
    wu_s = nc.dram_tensor("wu_s", [IC, 128, HB * 128], BF16).ap()
    wd_s = nc.dram_tensor("wd_s", [HB, 128, IC * 128], BF16).ap()

    sb = lambda name, shape, dt: ES.enter_context(nc.sbuf_tensor(name, shape, dt))
    resid_t = sb("resid", [128, HB * FW], F32)
    big_t = sb("big", [128, HB * FW // 2 + HB * FW], F32)
    xlnS_t = sb("xlnS", [128, HB * BL * W1], BF16)
    mixA_t = sb("mixA", [128, HB * FW], BF16)
    mixB_t = sb("mixB", [128, HB * FW], BF16)
    rsb_t = sb("rsb", [128, HB * FW], BF16)
    sgq_t = sb("sgq", [128, 2 * FW], BF16)
    sblk_t = sb("sblk", [128, BL * W1], F32)
    tmpf_t = sb("tmpf", [128, FW], F32)
    rows_t = sb("rows", [1, 4 * FW], F32)
    wbA_t = sb("wbA", [128, 2 * HB * 128], BF16)
    wbG_t = sb("wbG", [128, 2 * HB * 128], BF16)
    wbU_t = sb("wbU", [128, 2 * HB * 128], BF16)
    wbD_t = sb("wbD", [128, 2 * IC * 128], BF16)
    tdp_t = sb("tdp", [128, HB], F32)
    e1_t = sb("e1", [128, HB], F32)
    wdec_t = sb("wdec", [128, HB], F32)
    ufst_t = sb("ufst", [128, HB], F32)
    tmk_t = sb("tmk", [128, HB], F32)
    tmv_t = sb("tmv", [128, HB], F32)
    tmr_t = sb("tmr", [128, HB], F32)
    gsc_t = sb("gsc", [128, IC], F32)
    usc_t = sb("usc", [128, IC], F32)
    dsc_t = sb("dsc", [128, HB], F32)
    carx_t = sb("carx", [128, HB * BL], BF16)
    cars_t = sb("cars", [128, HB * BL], F32)
    onec_t = sb("onec", [128, 1], BF16)
    oner_t = sb("oner", [1, 128], F32)

    ps = [ES.enter_context(nc.psum_tensor(f"ps{i}", [128, FW], F32)) for i in range(4)]
    ps_m = ES.enter_context(nc.psum_tensor("ps_m", [128, FW], F32))
    ps_r = ES.enter_context(nc.psum_tensor("ps_r", [128, FW], F32))

    sem = {s: ES.enter_context(nc.semaphore(s)) for s in SEMS}

    resid = resid_t.ap()
    ksb = big_t.ap()[:, 0 : HB * FW // 2].bitcast(BF16)
    kvr = big_t.ap()[:, HB * FW // 2 :]
    hdn = big_t.ap()[:, 0 : (IC * FW) // 2].bitcast(BF16)
    xlnS = xlnS_t.ap().rearrange("p (h s t) -> p h s t", h=HB, s=BL)
    mixA = mixA_t.ap()
    mixB = mixB_t.ap()
    rsb = rsb_t.ap()
    sgq = sgq_t.ap()
    sblk = sblk_t.ap()
    tmpf = tmpf_t.ap()
    rows = rows_t.ap()
    mean_row = rows[:, 0:FW]
    msq_row = rows[:, FW : 2 * FW]
    var_row = rows[:, 2 * FW : 3 * FW]
    rstd_row = rows[:, 3 * FW : 4 * FW]
    wbA = wbA_t.ap()
    wbG = wbG_t.ap()
    wbU = wbU_t.ap()
    wbD = wbD_t.ap()
    carx = carx_t.ap()
    cars = cars_t.ap()
    onec = onec_t.ap()
    oner = oner_t.ap()
    ps_sa = ps_m.ap()[0:1, :]
    ps_sb = ps_r.ap()[0:1, :]

    def blk(ap, hb):
        return ap[:, hb * FW : (hb + 1) * FW]

    ENG = ("sync", "gpsimd", "vector", "scalar", "tensor")
    streams = {e: [] for e in ENG}
    cnt = {s: 0 for s in SEMS}
    lastw = {}

    def emit(e, fn):
        streams[e].append(fn)

    def wt(e, s, v):
        if v is None or v <= 0:
            return
        k = (e, s)
        if lastw.get(k, 0) >= v:
            return
        lastw[k] = v
        emit(e, lambda eng, S=sem[s], V=v: eng.wait_ge(S, V))

    def op(e, f, incsem=None):
        if incsem is None:
            emit(e, f)
            return None
        v = cnt[incsem] = cnt[incsem] + (16 if incsem in ("sQW", "sQW2", "sQX", "sCI", "sCO", "sOUT") else 1)
        n = 16 if incsem in ("sQW", "sQW2", "sQX", "sCI", "sCO", "sOUT") else 1
        emit(e, lambda eng, F=f, S=sem[incsem], N=n: F(eng).then_inc(S, N))
        return v

    p_loads = [(td_in, tdp_t.ap()), (tf_in, ufst_t.ap()), (tmk_in, tmk_t.ap()),
               (tmv_in, tmv_t.ap()), (tmr_in, tmr_t.ap()), (gs_in, gsc_t.ap()),
               (us_in, usc_t.ap()), (ds_in, dsc_t.ap())]
    par_ms = {}
    for j, (src, dst) in enumerate(p_loads):
        par_ms[j] = op("gpsimd", lambda e, s=src, d=dst: e.dma_start(out=d, in_=s), "sQX")

    wt("scalar", "sQX", par_ms[0])
    op("scalar", lambda e: e.activation(out=e1_t.ap(), in_=tdp_t.ap(), func=AF.Exp))
    wdec_ms = op("scalar", lambda e: e.activation(out=wdec_t.ap(), in_=e1_t.ap(), func=AF.Exp, scale=-1.0), "sACT")

    op("vector", lambda e: e.memset(onec, 1.0))
    op("vector", lambda e: e.memset(oner, 1.0))
    op("vector", lambda e: e.memset(carx, 0.0))
    init_ms = op("vector", lambda e: e.memset(cars, 0.0), "sVE")

    conv_steps = []
    for i in range(HB):
        conv_steps.append((wk_in[i], wk_s[i], HB * 128))
    for i in range(HB):
        conv_steps.append((wv_in[i], wv_s[i], HB * 128))
    for i in range(HB):
        conv_steps.append((wr_in[i], wr_s[i], HB * 128))
    for i in range(HB):
        conv_steps.append((wo_in[i], wo_s[i], HB * 128))
    for i in range(IC):
        conv_steps.append((wg_in[i], wg_s[i], HB * 128))
    for i in range(IC):
        conv_steps.append((wu_in[i], wu_s[i], HB * 128))
    for i in range(HB):
        conv_steps.append((wd_in[i], wd_s[i], IC * 128))
    scratch_ms = {}
    cin = [resid[:, 0:5632], kvr[:, 0:5632]]
    cout = [mixA[:, 0:5632], mixB[:, 0:5632]]
    cv_ms = []
    co_ms = []
    for j, (src, dst, ncols) in enumerate(conv_steps):
        nkb = ncols // 128
        if j >= 2:
            wt("sync", "sCV", cv_ms[j - 2])
        ci = op("sync", lambda e, s=src, d=cin[j % 2][:, 0:ncols].rearrange("p (kb o) -> p kb o", kb=nkb):
                e.dma_start(out=d, in_=s.rearrange("kb p o -> p kb o")), "sCI")
        wt("vector", "sCI", ci)
        if j >= 2:
            wt("vector", "sCO", co_ms[j - 2])
        cv_ms.append(op("vector", lambda e, a=cin[j % 2][:, 0:ncols], b=cout[j % 2][:, 0:ncols]:
                        e.tensor_copy(out=b, in_=a), "sCV"))
        wt("gpsimd", "sCV", cv_ms[j])
        co_ms.append(op("gpsimd", lambda e, d=dst.rearrange("p (kb o) -> p kb o", kb=nkb),
                        s=cout[j % 2][:, 0:ncols].rearrange("p (kb o) -> p kb o", kb=nkb):
                        e.dma_start(out=d, in_=s), "sCO"))
        scratch_ms[(id(dst.tensor), j)] = None
    conv_done_co = cnt["sCO"]
    slab_ms = {}
    for j, (src, dst, ncols) in enumerate(conv_steps):
        slab_ms[(dst.tensor.name, dst.offset)] = co_ms[j]

    wt("vector", "sCO", conv_done_co)
    wt("gpsimd", "sCO", conv_done_co)

    bank_free = {i: None for i in range(6)}

    def mm(psap, lh, rh, start, stop, incsem=None):
        return op("tensor", lambda e, P=psap, L=lh, R=rh, S=start, E=stop:
                  e.matmul(P, L, R, start=S, stop=E), incsem)

    def wait_bank(b):
        if bank_free[b] is not None:
            s, v = bank_free[b]
            wt("tensor", s, v)

    def wdma(dstap, srcslab, pending_free, qeng="sync"):
        if pending_free is not None:
            s, v = pending_free
            wt(qeng, s, v)
        key = (srcslab.tensor.name, srcslab.offset)
        wt(qeng, "sCO", slab_ms[key])
        qsem = "sQW" if qeng == "sync" else "sQW2"
        return (qsem, op(qeng, lambda e, d=dstap, s=srcslab: e.dma_start(out=d, in_=s), qsem))

    wbA_user = [None, None]
    wbG_user = [None, None]
    wbU_user = [None, None]
    wbD_user = [None, None]

    x_ms = [None] * NCH

    def emit_layernorm(save_carry):
        xb_ms = []
        for hb in range(HB):
            xb_ms.append(op("vector", lambda e, o=blk(mixA, hb), i=blk(resid, hb):
                            e.tensor_copy(out=o, in_=i), "sVE"))
        wait_bank(4)
        wait_bank(5)
        for hb in range(HB):
            wt("tensor", "sVE", xb_ms[hb])
            mm(ps_sa, onec, blk(mixA, hb), hb == 0, hb == HB - 1,
               "sPE" if hb == HB - 1 else None)
        sum_ms = cnt["sPE"]
        sq_pe = []
        for hb in range(HB):
            wt("scalar", "sVE", xb_ms[hb])
            if hb >= 2:
                wt("scalar", "sPE", sq_pe[hb - 2])
            a_ms = op("scalar", lambda e, o=sgq[:, (hb % 2) * FW:(hb % 2 + 1) * FW], i=blk(mixA, hb):
                      e.activation(out=o, in_=i, func=AF.Square), "sACT")
            wt("tensor", "sACT", a_ms)
            sq_pe.append(mm(ps_sb, onec, sgq[:, (hb % 2) * FW:(hb % 2 + 1) * FW],
                            hb == 0, hb == HB - 1, "sPE"))
        wt("vector", "sPE", sq_pe[-1])
        op("vector", lambda e: e.tensor_scalar_mul(mean_row, ps_sa, INV_H))
        op("vector", lambda e: e.tensor_mul(msq_row, mean_row, mean_row))
        op("vector", lambda e: e.scalar_tensor_tensor(
            out=var_row, in0=ps_sb, scalar=INV_H, in1=msq_row,
            op0=ALU.mult, op1=ALU.subtract))
        var_ms = op("vector", lambda e: e.tensor_scalar_add(var_row, var_row, EPS), "sVE")
        wt("scalar", "sVE", var_ms)
        sd_ms = op("scalar", lambda e: e.activation(out=msq_row, in_=var_row, func=AF.Sqrt), "sACT")
        wt("vector", "sACT", sd_ms)
        rstd_ms = op("vector", lambda e: e.reciprocal(out=rstd_row, in_=msq_row), "sVE")
        wt("tensor", "sVE", rstd_ms)
        bm_ms = mm(ps_m.ap(), oner, mean_row, True, True, "sPE")
        br_ms = mm(ps_r.ap(), oner, rstd_row, True, True, "sPE")
        wt("vector", "sPE", br_ms)
        norm_ms = []
        for hb in range(HB):
            for s in range(BL):
                op("vector", lambda e, o=xlnS[:, hb, s, 0:1], i=carx[:, hb * BL + s: hb * BL + s + 1]:
                   e.tensor_copy(out=o, in_=i))
            op("vector", lambda e, o=tmpf, a=blk(resid, hb), b=ps_m.ap():
               e.tensor_tensor(out=o, in0=a, in1=b, op=ALU.subtract))
            v = op("vector", lambda e, o=xlnS[:, hb, :, 1:W1], a=tmpf.rearrange("p (s t) -> p s t", s=BL),
                   b=ps_r.ap(): e.tensor_tensor(out=o, in0=a, in1=b.rearrange("p (s t) -> p s t", s=BL),
                                                op=ALU.mult), "sVE")
            norm_ms.append(v)
        if save_carry:
            for hb in range(HB):
                op("vector", lambda e, o=carx[:, hb * BL: (hb + 1) * BL],
                   i=xlnS[:, hb, :, W1 - 1: W1]:
                   e.tensor_copy(out=o, in_=i.rearrange("p s t -> p (s t)")))
        bank_free[4] = ("sVE", norm_ms[-1])
        bank_free[5] = ("sVE", norm_ms[-1])
        return norm_ms

    def emit_mix(dst, tm, interleave=None):
        ms = []
        for hb in range(HB):
            op("vector", lambda e, o=tmpf.rearrange("p (s t) -> p s t", s=BL),
               a=xlnS[:, hb, :, 1:W1], b=xlnS[:, hb, :, 0:TC]:
               e.tensor_tensor(out=o, in0=a, in1=b, op=ALU.subtract))
            ms.append(op("vector", lambda e, o=blk(dst, hb).rearrange("p (s t) -> p s t", s=BL),
                         a=tmpf.rearrange("p (s t) -> p s t", s=BL), sc=tm[:, hb:hb + 1],
                         b=xlnS[:, hb, :, 0:TC]:
                         e.scalar_tensor_tensor(out=o, in0=a, scalar=sc, in1=b,
                                                op0=ALU.mult, op1=ALU.add), "sVE"))
            if interleave is not None:
                interleave(hb)
        return ms

    def emit_proj(scratch, nout, rhs_blk, rhs_ready, wbuf, wbuf_user, nkb, evac):
        group_ms = []
        for oc in range(nout):
            buf = oc % 2
            w_ms = wdma(wbuf[:, buf * nkb * 128:(buf + 1) * nkb * 128], scratch[oc],
                        wbuf_user[buf], "sync" if oc % 2 == 0 else "scalar")
            bankid = oc % 2 if scratch is not wd_s else 2 + oc % 2
            wait_bank(bankid)
            psap = ps[bankid].ap()
            wt("tensor", w_ms[0], w_ms[1])
            for kb in range(nkb):
                if rhs_ready is not None and rhs_ready[kb] is not None:
                    s, v = rhs_ready[kb]
                    wt("tensor", s, v)
                mm(psap, wbuf[:, (buf * nkb + kb) * 128:(buf * nkb + kb + 1) * 128],
                   rhs_blk(kb), kb == 0, kb == nkb - 1, "sPE" if kb == nkb - 1 else None)
            g = cnt["sPE"]
            wbuf_user[buf] = ("sPE", g)
            ev = evac(oc, psap, g)
            bank_free[bankid] = ev
            group_ms.append(ev)
        return group_ms

    for c in range(NCH):
        for s in range(BL):
            dst = resid.rearrange("p (h s2 t) -> p h s2 t", h=HB, s2=BL)[:, :, s, :]
            x_ms[c] = op("gpsimd", lambda e, d=dst,
                         sr=x_in[s, c].rearrange("h p t -> p h t"):
                         e.dma_start(out=d, in_=sr), "sQX")
        wt("vector", "sQX", x_ms[c])

        norm1 = emit_layernorm(save_carry=True)
        rhs_ready_x = [("sVE", norm1[kb]) for kb in range(HB)]

        mixk_ms = emit_mix(mixA, tmk_t.ap())
        rr_k = [("sVE", mixk_ms[kb]) for kb in range(HB)]

        def kevac(oc, psap, g):
            wt("vector", "sPE", g)
            v = op("vector", lambda e, o=blk(ksb, oc), i=psap: e.tensor_copy(out=o, in_=i), "sVE")
            return ("sVE", v)

        kg = emit_proj(wk_s, HB, lambda kb: blk(mixA, kb), rr_k, wbA, wbA_user, HB, kevac)

        mixv_ms = emit_mix(mixB, tmv_t.ap())
        rr_v = [("sVE", mixv_ms[kb]) for kb in range(HB)]

        def vevac(oc, psap, g):
            wt("vector", "sPE", g)
            v = op("vector", lambda e, o=blk(kvr, oc), a=blk(ksb, oc), i=psap:
                   e.tensor_mul(o, a, i), "sVE")
            return ("sVE", v)

        vg = emit_proj(wv_s, HB, lambda kb: blk(mixB, kb), rr_v, wbA, wbA_user, HB, vevac)

        wt("vector", "sPE", wbA_user[(HB - 1) % 2][1])
        mixr_ms = emit_mix(mixA, tmr_t.ap())
        rr_r = [("sVE", mixr_ms[kb]) for kb in range(HB)]

        def revac(oc, psap, g):
            wt("scalar", "sPE", g)
            v = op("scalar", lambda e, o=blk(rsb, oc), i=psap:
                   e.activation(out=o, in_=i, func=AF.Sigmoid), "sACT")
            return ("sACT", v)

        rg = emit_proj(wr_s, HB, lambda kb: blk(mixA, kb), rr_r, wbA, wbA_user, HB, revac)

        wt("vector", "sACT", wdec_ms)
        wt("vector", "sPE", wbA_user[(HB - 1) % 2][1])
        wkv_ms = []
        for hb in range(HB):
            s_r, v_r = rg[hb]
            wt("vector", s_r, v_r)
            for s in range(BL):
                op("vector", lambda e, o=sblk[:, s * W1 + 1:(s + 1) * W1],
                   d0=wdec_t.ap()[:, hb:hb + 1].broadcast_to([128, TC]),
                   d1=kvr[:, hb * FW + s * TC: hb * FW + (s + 1) * TC],
                   init=cars[:, hb * BL + s: hb * BL + s + 1]:
                   e.tensor_tensor_scan(out=o, data0=d0, data1=d1, initial=init,
                                        op0=ALU.mult, op1=ALU.add))
                op("vector", lambda e, o=sblk[:, s * W1:s * W1 + 1],
                   i=cars[:, hb * BL + s: hb * BL + s + 1]:
                   e.tensor_copy(out=o, in_=i))
            op("vector", lambda e, o=tmpf.rearrange("p (s t) -> p s t", s=BL),
               a=blk(kvr, hb).rearrange("p (s t) -> p s t", s=BL), sc=ufst_t.ap()[:, hb:hb + 1],
               b=sblk.rearrange("p (s t) -> p s t", s=BL)[:, :, 0:TC]:
               e.scalar_tensor_tensor(out=o, in0=a, scalar=sc, in1=b, op0=ALU.mult, op1=ALU.add))
            wkv_ms.append(op("vector", lambda e, o=blk(mixB, hb), a=tmpf, b=blk(rsb, hb):
                             e.tensor_mul(o, a, b), "sVE"))
            for s in range(BL):
                op("vector", lambda e, o=cars[:, hb * BL + s: hb * BL + s + 1],
                   i=sblk[:, (s + 1) * W1 - 1:(s + 1) * W1]:
                   e.tensor_copy(out=o, in_=i))
        rr_wkv = [("sVE", wkv_ms[kb]) for kb in range(HB)]

        def woevac(oc, psap, g):
            wt("vector", "sPE", g)
            v = op("vector", lambda e, o=blk(resid, oc), i=psap:
                   e.tensor_add(o, o, i), "sVE")
            return ("sVE", v)

        og = emit_proj(wo_s, HB, lambda kb: blk(mixB, kb), rr_wkv, wbA, wbA_user, HB, woevac)

        wt("vector", "sPE", wbA_user[(HB - 1) % 2][1])
        for hb in range(HB):
            s_o, v_o = og[hb]
            wt("vector", s_o, v_o)
        norm2 = emit_layernorm(save_carry=False)
        rr_x2 = [("sVE", norm2[kb]) for kb in range(HB)]

        hdn_ms = []
        for ic in range(IC):
            bufg = ic % 2
            wg_ms = wdma(wbG[:, bufg * HB * 128:(bufg + 1) * HB * 128], wg_s[ic], wbG_user[bufg], "sync")
            wu_ms = wdma(wbU[:, bufg * HB * 128:(bufg + 1) * HB * 128], wu_s[ic], wbU_user[bufg], "scalar")
            bank_g = ic % 2
            wait_bank(bank_g)
            wt("tensor", wg_ms[0], wg_ms[1])
            for kb in range(HB):
                s_, v_ = rr_x2[kb]
                wt("tensor", s_, v_)
                mm(ps[bank_g].ap(), wbG[:, (bufg * HB + kb) * 128:(bufg * HB + kb + 1) * 128],
                   xlnS[:, kb, :, 1:W1], kb == 0, kb == HB - 1, "sPE" if kb == HB - 1 else None)
            g_pe = cnt["sPE"]
            wbG_user[bufg] = ("sPE", g_pe)
            bank_u = 2 + ic % 2
            wait_bank(bank_u)
            wt("tensor", wu_ms[0], wu_ms[1])
            for kb in range(HB):
                mm(ps[bank_u].ap(), wbU[:, (bufg * HB + kb) * 128:(bufg * HB + kb + 1) * 128],
                   xlnS[:, kb, :, 1:W1], kb == 0, kb == HB - 1, "sPE" if kb == HB - 1 else None)
            u_pe = cnt["sPE"]
            wbU_user[bufg] = ("sPE", u_pe)
            wt("scalar", "sPE", g_pe)
            if ic >= 2:
                wt("scalar", "sVE", hdn_ms[ic - 2])
            sg_ms = op("scalar", lambda e, o=sgq[:, bufg * FW:(bufg + 1) * FW], i=ps[bank_g].ap(),
                       sc=gsc_t.ap()[:, ic:ic + 1]:
                       e.activation(out=o, in_=i, func=AF.Silu, scale=sc), "sACT")
            bank_free[bank_g] = ("sACT", sg_ms)
            wt("vector", "sPE", u_pe)
            wt("vector", "sACT", sg_ms)
            h_ms = op("vector", lambda e, o=hdn[:, ic * FW:(ic + 1) * FW], i=ps[bank_u].ap(),
                      sc=usc_t.ap()[:, ic:ic + 1], b=sgq[:, bufg * FW:(bufg + 1) * FW]:
                      e.scalar_tensor_tensor(out=o, in0=i, scalar=sc, in1=b,
                                             op0=ALU.mult, op1=ALU.mult), "sVE")
            bank_free[bank_u] = ("sVE", h_ms)
            hdn_ms.append(h_ms)

        down_last = None
        for hc in range(HB):
            bufd = hc % 2
            wd_ms = wdma(wbD[:, bufd * IC * 128:(bufd + 1) * IC * 128], wd_s[hc],
                         wbD_user[bufd], "sync" if hc % 2 == 0 else "scalar")
            bank_d = 2 + hc % 2
            wait_bank(bank_d)
            wt("tensor", wd_ms[0], wd_ms[1])
            for kb in range(IC):
                wt("tensor", "sVE", hdn_ms[kb])
                mm(ps[bank_d].ap(), wbD[:, (bufd * IC + kb) * 128:(bufd * IC + kb + 1) * 128],
                   hdn[:, kb * FW:(kb + 1) * FW], kb == 0, kb == IC - 1,
                   "sPE" if kb == IC - 1 else None)
            d_pe = cnt["sPE"]
            wbD_user[bufd] = ("sPE", d_pe)
            wt("vector", "sPE", d_pe)
            down_last = op("vector", lambda e, o=blk(resid, hc), i=ps[bank_d].ap(),
                           sc=dsc_t.ap()[:, hc:hc + 1]:
                           e.scalar_tensor_tensor(out=o, in0=i, scalar=sc, in1=o,
                                                  op0=ALU.mult, op1=ALU.add), "sVE")
            bank_free[bank_d] = ("sVE", down_last)

        wt("gpsimd", "sVE", down_last)
        st_ms = None
        for s in range(BL):
            src = resid.rearrange("p (h s2 t) -> p h s2 t", h=HB, s2=BL)[:, :, s, :]
            st_ms = op("gpsimd", lambda e, sr=src,
                       d=y_out[s, c].rearrange("h p t -> p h t"):
                       e.dma_start(out=d, in_=sr), "sOUT")
        wt("gpsimd", "sOUT", st_ms)


    with nc.Block() as block:
        @block.sync
        def _(eng):
            for f in streams["sync"]:
                f(eng)

        @block.gpsimd
        def _(eng):
            for f in streams["gpsimd"]:
                f(eng)

        @block.vector
        def _(eng):
            for f in streams["vector"]:
                f(eng)

        @block.scalar
        def _(eng):
            for f in streams["scalar"]:
                f(eng)

        @block.tensor
        def _(eng):
            for f in streams["tensor"]:
                f(eng)

    nc._keepalive_es = ES
    return nc


def _tile_w(wT, nout, nkb):
    Kd, Md = wT.shape
    a = wT.reshape(nkb, 128, nout, 128).transpose(2, 0, 1, 3)
    return np.ascontiguousarray(a, dtype=np.float32)


def kernel(L0_attn_hidden_states, L0_attn_Wr, L0_attn_Wk, L0_attn_Wv, L0_attn_Wo,
           L0_attn_time_decay, L0_attn_time_first, L0_attn_time_mix_k,
           L0_attn_time_mix_v, L0_attn_time_mix_r, L0_ffn_gate_w, L0_ffn_up_w,
           L0_ffn_down_w, L0_ffn_gate_s, L0_ffn_up_s, L0_ffn_down_s):
    x = np.asarray(L0_attn_hidden_states, dtype=np.float32)
    xt = np.ascontiguousarray(
        x.reshape(B, NCH, TC, HB, 128).transpose(0, 1, 3, 4, 2))

    wk = _tile_w(np.asarray(L0_attn_Wk).T, HB, HB)
    wv = _tile_w(np.asarray(L0_attn_Wv).T, HB, HB)
    wr = _tile_w(np.asarray(L0_attn_Wr).T, HB, HB)
    wo = _tile_w(np.asarray(L0_attn_Wo).T, HB, HB)
    wg = _tile_w(np.asarray(L0_ffn_gate_w).T, IC, HB)
    wu = _tile_w(np.asarray(L0_ffn_up_w).T, IC, HB)
    wd = _tile_w(np.asarray(L0_ffn_down_w).T, HB, IC)

    f32 = lambda a: np.ascontiguousarray(np.asarray(a, dtype=np.float32))
    td = f32(np.asarray(L0_attn_time_decay).reshape(HB, 128).T)
    tf = f32(np.asarray(L0_attn_time_first).reshape(HB, 128).T)
    tmk = f32(np.asarray(L0_attn_time_mix_k).reshape(HB, 128).T)
    tmv = f32(np.asarray(L0_attn_time_mix_v).reshape(HB, 128).T)
    tmr = f32(np.asarray(L0_attn_time_mix_r).reshape(HB, 128).T)
    gs = f32(np.asarray(L0_ffn_gate_s).reshape(IC, 128).T)
    us = f32(np.asarray(L0_ffn_up_s).reshape(IC, 128).T)
    ds = f32(np.asarray(L0_ffn_down_s).reshape(HB, 128).T)

    nc = build_nc()
    in_maps = []
    for core in range(NCORE):
        in_maps.append({
            "x_in": xt[core * BL:(core + 1) * BL],
            "wk_in": wk, "wv_in": wv, "wr_in": wr, "wo_in": wo,
            "wg_in": wg, "wu_in": wu, "wd_in": wd,
            "td_in": td, "tf_in": tf, "tmk_in": tmk, "tmv_in": tmv, "tmr_in": tmr,
            "gs_in": gs, "us_in": us, "ds_in": ds,
        })
    res = run_bass_kernel_spmd(nc, in_maps, core_ids=list(range(NCORE)))
    outs = []
    for core in range(NCORE):
        yc = res.results[core]["y_out"]
        outs.append(yc.transpose(0, 1, 4, 2, 3).reshape(BL, T, H))
    return np.ascontiguousarray(np.concatenate(outs, axis=0), dtype=np.float32)


# version 05
# speedup vs baseline: 1.0481x; 1.0481x over previous
import contextlib
import numpy as np
import concourse.bass as bass
import concourse.mybir as mybir
from concourse.bass_utils import run_bass_kernel_spmd

F32 = mybir.dt.float32
BF16 = mybir.dt.bfloat16
AF = mybir.ActivationFunctionType
ALU = mybir.AluOpType

B, T, H = 16, 2048, 2048
NCORE = 8
BL = B // NCORE
TC = 256
NCH = T // TC
FW = BL * TC
HB = H // 128
I = 5632
IC = I // 128
W1 = TC + 1
INV_H = 1.0 / H
EPS = 1e-5

SEMS = ("sQW", "sQX", "sPE", "sVE", "sACT", "sOUT", "sCI", "sCV", "sCO")


def build_nc():
    nc = bass.Bass("TRN2", target_bir_lowering=False)
    ES = contextlib.ExitStack()

    x_in = nc.dram_tensor("x_in", [BL, NCH, HB, 128, TC], F32, kind="ExternalInput").ap()
    y_out = nc.dram_tensor("y_out", [BL, NCH, HB, 128, TC], F32, kind="ExternalOutput").ap()
    wk_in = nc.dram_tensor("wk_in", [HB, HB, 128, 128], F32, kind="ExternalInput").ap()
    wv_in = nc.dram_tensor("wv_in", [HB, HB, 128, 128], F32, kind="ExternalInput").ap()
    wr_in = nc.dram_tensor("wr_in", [HB, HB, 128, 128], F32, kind="ExternalInput").ap()
    wo_in = nc.dram_tensor("wo_in", [HB, HB, 128, 128], F32, kind="ExternalInput").ap()
    wg_in = nc.dram_tensor("wg_in", [IC, HB, 128, 128], F32, kind="ExternalInput").ap()
    wu_in = nc.dram_tensor("wu_in", [IC, HB, 128, 128], F32, kind="ExternalInput").ap()
    wd_in = nc.dram_tensor("wd_in", [HB, IC, 128, 128], F32, kind="ExternalInput").ap()
    td_in = nc.dram_tensor("td_in", [128, HB], F32, kind="ExternalInput").ap()
    tf_in = nc.dram_tensor("tf_in", [128, HB], F32, kind="ExternalInput").ap()
    tmk_in = nc.dram_tensor("tmk_in", [128, HB], F32, kind="ExternalInput").ap()
    tmv_in = nc.dram_tensor("tmv_in", [128, HB], F32, kind="ExternalInput").ap()
    tmr_in = nc.dram_tensor("tmr_in", [128, HB], F32, kind="ExternalInput").ap()
    gs_in = nc.dram_tensor("gs_in", [128, IC], F32, kind="ExternalInput").ap()
    us_in = nc.dram_tensor("us_in", [128, IC], F32, kind="ExternalInput").ap()
    ds_in = nc.dram_tensor("ds_in", [128, HB], F32, kind="ExternalInput").ap()
    wk_s = nc.dram_tensor("wk_s", [HB, 128, HB * 128], BF16).ap()
    wv_s = nc.dram_tensor("wv_s", [HB, 128, HB * 128], BF16).ap()
    wr_s = nc.dram_tensor("wr_s", [HB, 128, HB * 128], BF16).ap()
    wo_s = nc.dram_tensor("wo_s", [HB, 128, HB * 128], BF16).ap()
    wg_s = nc.dram_tensor("wg_s", [IC, 128, HB * 128], BF16).ap()
    wu_s = nc.dram_tensor("wu_s", [IC, 128, HB * 128], BF16).ap()
    wd_s = nc.dram_tensor("wd_s", [HB, 128, IC * 128], BF16).ap()

    sb = lambda name, shape, dt: ES.enter_context(nc.sbuf_tensor(name, shape, dt))
    resid_t = sb("resid", [128, HB * FW], F32)
    big_t = sb("big", [128, HB * FW // 2 + HB * FW], F32)
    xlnS_t = sb("xlnS", [128, HB * BL * W1], BF16)
    mixA_t = sb("mixA", [128, HB * FW], BF16)
    mixB_t = sb("mixB", [128, HB * FW], BF16)
    rsb_t = sb("rsb", [128, HB * FW], BF16)
    sgq_t = sb("sgq", [128, 2 * FW], BF16)
    sblk_t = sb("sblk", [128, BL * W1], F32)
    tmpf_t = sb("tmpf", [128, FW], F32)
    rows_t = sb("rows", [1, 3 * FW], F32)
    wbA_t = sb("wbA", [128, 3 * HB * 128], BF16)
    wbG_t = sb("wbG", [128, 2 * HB * 128], BF16)
    wbU_t = sb("wbU", [128, 2 * HB * 128], BF16)
    wbD_t = sb("wbD", [128, 2 * IC * 128], BF16)
    tdp_t = sb("tdp", [128, HB], F32)
    e1_t = sb("e1", [128, HB], F32)
    wdec_t = sb("wdec", [128, HB], F32)
    ufst_t = sb("ufst", [128, HB], F32)
    tmk_t = sb("tmk", [128, HB], F32)
    tmv_t = sb("tmv", [128, HB], F32)
    tmr_t = sb("tmr", [128, HB], F32)
    gsc_t = sb("gsc", [128, IC], F32)
    usc_t = sb("usc", [128, IC], F32)
    dsc_t = sb("dsc", [128, HB], F32)
    carx_t = sb("carx", [128, HB * BL], BF16)
    cars_t = sb("cars", [128, HB * BL], F32)
    onec_t = sb("onec", [128, 1], BF16)
    oner_t = sb("oner", [1, 128], F32)

    ps = [ES.enter_context(nc.psum_tensor(f"ps{i}", [128, FW], F32)) for i in range(4)]
    ps_m = ES.enter_context(nc.psum_tensor("ps_m", [128, FW], F32))
    ps_r = ES.enter_context(nc.psum_tensor("ps_r", [128, FW], F32))

    sem = {s: ES.enter_context(nc.semaphore(s)) for s in SEMS}

    resid = resid_t.ap()
    ksb = big_t.ap()[:, 0 : HB * FW // 2].bitcast(BF16)
    kvr = big_t.ap()[:, HB * FW // 2 :]
    hdn = big_t.ap()[:, 0 : (IC * FW) // 2].bitcast(BF16)
    xlnS = xlnS_t.ap().rearrange("p (h s t) -> p h s t", h=HB, s=BL)
    mixA = mixA_t.ap()
    mixB = mixB_t.ap()
    rsb = rsb_t.ap()
    sgq = sgq_t.ap()
    sblk = sblk_t.ap()
    tmpf = tmpf_t.ap()
    rows = rows_t.ap()
    mean_row = rows[:, 0:FW]
    msq_row = rows[:, FW : 2 * FW]
    var_row = rows[:, 2 * FW : 3 * FW]
    rstd_row = var_row
    wbA = wbA_t.ap()
    wbG = wbG_t.ap()
    wbU = wbU_t.ap()
    wbD = wbD_t.ap()
    carx = carx_t.ap()
    cars = cars_t.ap()
    onec = onec_t.ap()
    oner = oner_t.ap()
    ps_sa = ps_m.ap()[0:1, :]
    ps_sb = ps_r.ap()[0:1, :]

    def blk(ap, hb):
        return ap[:, hb * FW : (hb + 1) * FW]

    ENG = ("sync", "gpsimd", "vector", "scalar", "tensor")
    streams = {e: [] for e in ENG}
    cnt = {s: 0 for s in SEMS}
    lastw = {}

    def emit(e, fn):
        streams[e].append(fn)

    def wt(e, s, v):
        if v is None or v <= 0:
            return
        k = (e, s)
        if lastw.get(k, 0) >= v:
            return
        lastw[k] = v
        emit(e, lambda eng, S=sem[s], V=v: eng.wait_ge(S, V))

    def op(e, f, incsem=None):
        if incsem is None:
            emit(e, f)
            return None
        v = cnt[incsem] = cnt[incsem] + (16 if incsem in ("sQW", "sQX", "sCI", "sCO", "sOUT") else 1)
        n = 16 if incsem in ("sQW", "sQX", "sCI", "sCO", "sOUT") else 1
        emit(e, lambda eng, F=f, S=sem[incsem], N=n: F(eng).then_inc(S, N))
        return v

    p_loads = [(td_in, tdp_t.ap()), (tf_in, ufst_t.ap()), (tmk_in, tmk_t.ap()),
               (tmv_in, tmv_t.ap()), (tmr_in, tmr_t.ap()), (gs_in, gsc_t.ap()),
               (us_in, usc_t.ap()), (ds_in, dsc_t.ap())]
    par_ms = {}
    for j, (src, dst) in enumerate(p_loads):
        par_ms[j] = op("gpsimd", lambda e, s=src, d=dst: e.dma_start(out=d, in_=s), "sQX")

    wt("scalar", "sQX", par_ms[0])
    op("scalar", lambda e: e.activation(out=e1_t.ap(), in_=tdp_t.ap(), func=AF.Exp))
    wdec_ms = op("scalar", lambda e: e.activation(out=wdec_t.ap(), in_=e1_t.ap(), func=AF.Exp, scale=-1.0), "sACT")

    op("vector", lambda e: e.memset(onec, 1.0))
    op("vector", lambda e: e.memset(oner, 1.0))
    op("vector", lambda e: e.memset(carx, 0.0))
    init_ms = op("vector", lambda e: e.memset(cars, 0.0), "sVE")

    conv_steps = []
    for i in range(HB):
        conv_steps.append((wk_in[i], wk_s[i], HB * 128))
    for i in range(HB):
        conv_steps.append((wv_in[i], wv_s[i], HB * 128))
    for i in range(HB):
        conv_steps.append((wr_in[i], wr_s[i], HB * 128))
    for i in range(HB):
        conv_steps.append((wo_in[i], wo_s[i], HB * 128))
    for i in range(IC):
        conv_steps.append((wg_in[i], wg_s[i], HB * 128))
    for i in range(IC):
        conv_steps.append((wu_in[i], wu_s[i], HB * 128))
    for i in range(HB):
        conv_steps.append((wd_in[i], wd_s[i], IC * 128))
    scratch_ms = {}
    cin = [resid[:, 0:5632], kvr[:, 0:5632]]
    cout = [mixA[:, 0:5632], mixB[:, 0:5632]]
    cv_ms = []
    co_ms = []
    for j, (src, dst, ncols) in enumerate(conv_steps):
        nkb = ncols // 128
        if j >= 2:
            wt("sync", "sCV", cv_ms[j - 2])
        ci = op("sync", lambda e, s=src, d=cin[j % 2][:, 0:ncols].rearrange("p (kb o) -> p kb o", kb=nkb):
                e.dma_start(out=d, in_=s.rearrange("kb p o -> p kb o")), "sCI")
        wt("vector", "sCI", ci)
        if j >= 2:
            wt("vector", "sCO", co_ms[j - 2])
        cv_ms.append(op("vector", lambda e, a=cin[j % 2][:, 0:ncols], b=cout[j % 2][:, 0:ncols]:
                        e.tensor_copy(out=b, in_=a), "sCV"))
        wt("gpsimd", "sCV", cv_ms[j])
        co_ms.append(op("gpsimd", lambda e, d=dst.rearrange("p (kb o) -> p kb o", kb=nkb),
                        s=cout[j % 2][:, 0:ncols].rearrange("p (kb o) -> p kb o", kb=nkb):
                        e.dma_start(out=d, in_=s), "sCO"))
        scratch_ms[(id(dst.tensor), j)] = None
    conv_done_co = cnt["sCO"]
    slab_ms = {}
    for j, (src, dst, ncols) in enumerate(conv_steps):
        slab_ms[(dst.tensor.name, dst.offset)] = co_ms[j]

    wt("vector", "sCO", conv_done_co)
    wt("gpsimd", "sCO", conv_done_co)

    bank_free = {i: None for i in range(6)}

    def mm(psap, lh, rh, start, stop, incsem=None):
        return op("tensor", lambda e, P=psap, L=lh, R=rh, S=start, E=stop:
                  e.matmul(P, L, R, start=S, stop=E), incsem)

    def wait_bank(b):
        if bank_free[b] is not None:
            s, v = bank_free[b]
            wt("tensor", s, v)

    def wdma(dstap, srcslab, pending_free):
        if pending_free is not None:
            s, v = pending_free
            wt("sync", s, v)
        key = (srcslab.tensor.name, srcslab.offset)
        wt("sync", "sCO", slab_ms[key])
        return op("sync", lambda e, d=dstap, s=srcslab: e.dma_start(out=d, in_=s), "sQW")

    wbA_user = [None, None, None]
    wbuf_ctr = {}
    wbG_user = [None, None]
    wbU_user = [None, None]
    wbD_user = [None, None]

    x_ms = [None] * NCH

    def emit_layernorm(save_carry):
        xb_ms = []
        for hb in range(HB):
            xb_ms.append(op("vector", lambda e, o=blk(mixA, hb), i=blk(resid, hb):
                            e.tensor_copy(out=o, in_=i), "sVE"))
        wait_bank(4)
        wait_bank(5)
        for hb in range(HB):
            wt("tensor", "sVE", xb_ms[hb])
            mm(ps_sa, onec, blk(mixA, hb), hb == 0, hb == HB - 1,
               "sPE" if hb == HB - 1 else None)
        sum_ms = cnt["sPE"]
        sq_pe = []
        for hb in range(HB):
            wt("scalar", "sVE", xb_ms[hb])
            if hb >= 2:
                wt("scalar", "sPE", sq_pe[hb - 2])
            a_ms = op("scalar", lambda e, o=sgq[:, (hb % 2) * FW:(hb % 2 + 1) * FW], i=blk(mixA, hb):
                      e.activation(out=o, in_=i, func=AF.Square), "sACT")
            wt("tensor", "sACT", a_ms)
            sq_pe.append(mm(ps_sb, onec, sgq[:, (hb % 2) * FW:(hb % 2 + 1) * FW],
                            hb == 0, hb == HB - 1, "sPE"))
        wt("vector", "sPE", sq_pe[-1])
        op("vector", lambda e: e.tensor_scalar_mul(mean_row, ps_sa, INV_H))
        op("vector", lambda e: e.tensor_mul(msq_row, mean_row, mean_row))
        op("vector", lambda e: e.scalar_tensor_tensor(
            out=var_row, in0=ps_sb, scalar=INV_H, in1=msq_row,
            op0=ALU.mult, op1=ALU.subtract))
        var_ms = op("vector", lambda e: e.tensor_scalar_add(var_row, var_row, EPS), "sVE")
        wt("scalar", "sVE", var_ms)
        sd_ms = op("scalar", lambda e: e.activation(out=msq_row, in_=var_row, func=AF.Sqrt), "sACT")
        wt("vector", "sACT", sd_ms)
        rstd_ms = op("vector", lambda e: e.reciprocal(out=rstd_row, in_=msq_row), "sVE")
        wt("tensor", "sVE", rstd_ms)
        bm_ms = mm(ps_m.ap(), oner, mean_row, True, True, "sPE")
        br_ms = mm(ps_r.ap(), oner, rstd_row, True, True, "sPE")
        wt("vector", "sPE", br_ms)
        norm_ms = []
        for hb in range(HB):
            for s in range(BL):
                op("vector", lambda e, o=xlnS[:, hb, s, 0:1], i=carx[:, hb * BL + s: hb * BL + s + 1]:
                   e.tensor_copy(out=o, in_=i))
            op("vector", lambda e, o=tmpf, a=blk(resid, hb), b=ps_m.ap():
               e.tensor_tensor(out=o, in0=a, in1=b, op=ALU.subtract))
            v = op("vector", lambda e, o=xlnS[:, hb, :, 1:W1], a=tmpf.rearrange("p (s t) -> p s t", s=BL),
                   b=ps_r.ap(): e.tensor_tensor(out=o, in0=a, in1=b.rearrange("p (s t) -> p s t", s=BL),
                                                op=ALU.mult), "sVE")
            norm_ms.append(v)
        if save_carry:
            for hb in range(HB):
                op("vector", lambda e, o=carx[:, hb * BL: (hb + 1) * BL],
                   i=xlnS[:, hb, :, W1 - 1: W1]:
                   e.tensor_copy(out=o, in_=i.rearrange("p s t -> p (s t)")))
        bank_free[4] = ("sVE", norm_ms[-1])
        bank_free[5] = ("sVE", norm_ms[-1])
        return norm_ms

    def emit_mix(dst, tm, interleave=None):
        ms = []
        for hb in range(HB):
            op("vector", lambda e, o=tmpf.rearrange("p (s t) -> p s t", s=BL),
               a=xlnS[:, hb, :, 1:W1], b=xlnS[:, hb, :, 0:TC]:
               e.tensor_tensor(out=o, in0=a, in1=b, op=ALU.subtract))
            ms.append(op("vector", lambda e, o=blk(dst, hb).rearrange("p (s t) -> p s t", s=BL),
                         a=tmpf.rearrange("p (s t) -> p s t", s=BL), sc=tm[:, hb:hb + 1],
                         b=xlnS[:, hb, :, 0:TC]:
                         e.scalar_tensor_tensor(out=o, in0=a, scalar=sc, in1=b,
                                                op0=ALU.mult, op1=ALU.add), "sVE"))
            if interleave is not None:
                interleave(hb)
        return ms

    def emit_proj(scratch, nout, rhs_blk, rhs_ready, wbuf, wbuf_user, nkb, evac):
        group_ms = []
        nbuf = len(wbuf_user)
        key = id(wbuf_user)
        for oc in range(nout):
            buf = wbuf_ctr.get(key, 0) % nbuf
            wbuf_ctr[key] = wbuf_ctr.get(key, 0) + 1
            w_ms = wdma(wbuf[:, buf * nkb * 128:(buf + 1) * nkb * 128], scratch[oc], wbuf_user[buf])
            bankid = oc % 2 if scratch is not wd_s else 2 + oc % 2
            wait_bank(bankid)
            psap = ps[bankid].ap()
            wt("tensor", "sQW", w_ms)
            for kb in range(nkb):
                if rhs_ready is not None and rhs_ready[kb] is not None:
                    s, v = rhs_ready[kb]
                    wt("tensor", s, v)
                mm(psap, wbuf[:, (buf * nkb + kb) * 128:(buf * nkb + kb + 1) * 128],
                   rhs_blk(kb), kb == 0, kb == nkb - 1, "sPE" if kb == nkb - 1 else None)
            g = cnt["sPE"]
            wbuf_user[buf] = ("sPE", g)
            ev = evac(oc, psap, g)
            bank_free[bankid] = ev
            group_ms.append(ev)
        return group_ms, g

    for c in range(NCH):
        for s in range(BL):
            dst = resid.rearrange("p (h s2 t) -> p h s2 t", h=HB, s2=BL)[:, :, s, :]
            x_ms[c] = op("gpsimd", lambda e, d=dst,
                         sr=x_in[s, c].rearrange("h p t -> p h t"):
                         e.dma_start(out=d, in_=sr), "sQX")
        wt("vector", "sQX", x_ms[c])

        norm1 = emit_layernorm(save_carry=True)
        rhs_ready_x = [("sVE", norm1[kb]) for kb in range(HB)]

        mixk_ms = emit_mix(mixA, tmk_t.ap())
        rr_k = [("sVE", mixk_ms[kb]) for kb in range(HB)]

        def kevac(oc, psap, g):
            wt("vector", "sPE", g)
            v = op("vector", lambda e, o=blk(ksb, oc), i=psap: e.tensor_copy(out=o, in_=i), "sVE")
            return ("sVE", v)

        kg, kg_pe = emit_proj(wk_s, HB, lambda kb: blk(mixA, kb), rr_k, wbA, wbA_user, HB, kevac)

        mixv_ms = emit_mix(mixB, tmv_t.ap())
        rr_v = [("sVE", mixv_ms[kb]) for kb in range(HB)]

        def vevac(oc, psap, g):
            wt("vector", "sPE", g)
            v = op("vector", lambda e, o=blk(kvr, oc), a=blk(ksb, oc), i=psap:
                   e.tensor_mul(o, a, i), "sVE")
            return ("sVE", v)

        vg, vg_pe = emit_proj(wv_s, HB, lambda kb: blk(mixB, kb), rr_v, wbA, wbA_user, HB, vevac)

        wt("vector", "sPE", vg_pe)
        mixr_ms = emit_mix(mixA, tmr_t.ap())
        rr_r = [("sVE", mixr_ms[kb]) for kb in range(HB)]

        def revac(oc, psap, g):
            wt("scalar", "sPE", g)
            v = op("scalar", lambda e, o=blk(rsb, oc), i=psap:
                   e.activation(out=o, in_=i, func=AF.Sigmoid), "sACT")
            return ("sACT", v)

        rg, rg_pe = emit_proj(wr_s, HB, lambda kb: blk(mixA, kb), rr_r, wbA, wbA_user, HB, revac)

        wt("vector", "sACT", wdec_ms)
        wt("vector", "sPE", rg_pe)
        wkv_ms = []
        for hb in range(HB):
            s_r, v_r = rg[hb]
            wt("vector", s_r, v_r)
            for s in range(BL):
                op("vector", lambda e, o=sblk[:, s * W1 + 1:(s + 1) * W1],
                   d0=wdec_t.ap()[:, hb:hb + 1].broadcast_to([128, TC]),
                   d1=kvr[:, hb * FW + s * TC: hb * FW + (s + 1) * TC],
                   init=cars[:, hb * BL + s: hb * BL + s + 1]:
                   e.tensor_tensor_scan(out=o, data0=d0, data1=d1, initial=init,
                                        op0=ALU.mult, op1=ALU.add))
                op("vector", lambda e, o=sblk[:, s * W1:s * W1 + 1],
                   i=cars[:, hb * BL + s: hb * BL + s + 1]:
                   e.tensor_copy(out=o, in_=i))
            op("vector", lambda e, o=tmpf.rearrange("p (s t) -> p s t", s=BL),
               a=blk(kvr, hb).rearrange("p (s t) -> p s t", s=BL), sc=ufst_t.ap()[:, hb:hb + 1],
               b=sblk.rearrange("p (s t) -> p s t", s=BL)[:, :, 0:TC]:
               e.scalar_tensor_tensor(out=o, in0=a, scalar=sc, in1=b, op0=ALU.mult, op1=ALU.add))
            wkv_ms.append(op("vector", lambda e, o=blk(mixB, hb), a=tmpf, b=blk(rsb, hb):
                             e.tensor_mul(o, a, b), "sVE"))
            for s in range(BL):
                op("vector", lambda e, o=cars[:, hb * BL + s: hb * BL + s + 1],
                   i=sblk[:, (s + 1) * W1 - 1:(s + 1) * W1]:
                   e.tensor_copy(out=o, in_=i))
        rr_wkv = [("sVE", wkv_ms[kb]) for kb in range(HB)]

        def woevac(oc, psap, g):
            wt("vector", "sPE", g)
            v = op("vector", lambda e, o=blk(resid, oc), i=psap:
                   e.tensor_add(o, o, i), "sVE")
            return ("sVE", v)

        og, og_pe = emit_proj(wo_s, HB, lambda kb: blk(mixB, kb), rr_wkv, wbA, wbA_user, HB, woevac)

        wt("vector", "sPE", og_pe)
        for hb in range(HB):
            s_o, v_o = og[hb]
            wt("vector", s_o, v_o)
        norm2 = emit_layernorm(save_carry=False)
        rr_x2 = [("sVE", norm2[kb]) for kb in range(HB)]

        hdn_ms = []
        for ic in range(IC):
            bufg = ic % 2
            wg_ms = wdma(wbG[:, bufg * HB * 128:(bufg + 1) * HB * 128], wg_s[ic], wbG_user[bufg])
            wu_ms = wdma(wbU[:, bufg * HB * 128:(bufg + 1) * HB * 128], wu_s[ic], wbU_user[bufg])
            bank_g = ic % 2
            wait_bank(bank_g)
            wt("tensor", "sQW", wg_ms)
            for kb in range(HB):
                s_, v_ = rr_x2[kb]
                wt("tensor", s_, v_)
                mm(ps[bank_g].ap(), wbG[:, (bufg * HB + kb) * 128:(bufg * HB + kb + 1) * 128],
                   xlnS[:, kb, :, 1:W1], kb == 0, kb == HB - 1, "sPE" if kb == HB - 1 else None)
            g_pe = cnt["sPE"]
            wbG_user[bufg] = ("sPE", g_pe)
            bank_u = 2 + ic % 2
            wait_bank(bank_u)
            wt("tensor", "sQW", wu_ms)
            for kb in range(HB):
                mm(ps[bank_u].ap(), wbU[:, (bufg * HB + kb) * 128:(bufg * HB + kb + 1) * 128],
                   xlnS[:, kb, :, 1:W1], kb == 0, kb == HB - 1, "sPE" if kb == HB - 1 else None)
            u_pe = cnt["sPE"]
            wbU_user[bufg] = ("sPE", u_pe)
            wt("scalar", "sPE", g_pe)
            if ic >= 2:
                wt("scalar", "sVE", hdn_ms[ic - 2])
            sg_ms = op("scalar", lambda e, o=sgq[:, bufg * FW:(bufg + 1) * FW], i=ps[bank_g].ap(),
                       sc=gsc_t.ap()[:, ic:ic + 1]:
                       e.activation(out=o, in_=i, func=AF.Silu, scale=sc), "sACT")
            bank_free[bank_g] = ("sACT", sg_ms)
            wt("vector", "sPE", u_pe)
            wt("vector", "sACT", sg_ms)
            h_ms = op("vector", lambda e, o=hdn[:, ic * FW:(ic + 1) * FW], i=ps[bank_u].ap(),
                      sc=usc_t.ap()[:, ic:ic + 1], b=sgq[:, bufg * FW:(bufg + 1) * FW]:
                      e.scalar_tensor_tensor(out=o, in0=i, scalar=sc, in1=b,
                                             op0=ALU.mult, op1=ALU.mult), "sVE")
            bank_free[bank_u] = ("sVE", h_ms)
            hdn_ms.append(h_ms)

        down_last = None
        for hc in range(HB):
            bufd = hc % 2
            wd_ms = wdma(wbD[:, bufd * IC * 128:(bufd + 1) * IC * 128], wd_s[hc], wbD_user[bufd])
            bank_d = 2 + hc % 2
            wait_bank(bank_d)
            wt("tensor", "sQW", wd_ms)
            for kb in range(IC):
                wt("tensor", "sVE", hdn_ms[kb])
                mm(ps[bank_d].ap(), wbD[:, (bufd * IC + kb) * 128:(bufd * IC + kb + 1) * 128],
                   hdn[:, kb * FW:(kb + 1) * FW], kb == 0, kb == IC - 1,
                   "sPE" if kb == IC - 1 else None)
            d_pe = cnt["sPE"]
            wbD_user[bufd] = ("sPE", d_pe)
            wt("vector", "sPE", d_pe)
            down_last = op("vector", lambda e, o=blk(resid, hc), i=ps[bank_d].ap(),
                           sc=dsc_t.ap()[:, hc:hc + 1]:
                           e.scalar_tensor_tensor(out=o, in0=i, scalar=sc, in1=o,
                                                  op0=ALU.mult, op1=ALU.add), "sVE")
            bank_free[bank_d] = ("sVE", down_last)

        wt("gpsimd", "sVE", down_last)
        st_ms = None
        for s in range(BL):
            src = resid.rearrange("p (h s2 t) -> p h s2 t", h=HB, s2=BL)[:, :, s, :]
            st_ms = op("gpsimd", lambda e, sr=src,
                       d=y_out[s, c].rearrange("h p t -> p h t"):
                       e.dma_start(out=d, in_=sr), "sOUT")
        wt("gpsimd", "sOUT", st_ms)


    with nc.Block() as block:
        @block.sync
        def _(eng):
            for f in streams["sync"]:
                f(eng)

        @block.gpsimd
        def _(eng):
            for f in streams["gpsimd"]:
                f(eng)

        @block.vector
        def _(eng):
            for f in streams["vector"]:
                f(eng)

        @block.scalar
        def _(eng):
            for f in streams["scalar"]:
                f(eng)

        @block.tensor
        def _(eng):
            for f in streams["tensor"]:
                f(eng)

    nc._keepalive_es = ES
    return nc


def _tile_w(wT, nout, nkb):
    Kd, Md = wT.shape
    a = wT.reshape(nkb, 128, nout, 128).transpose(2, 0, 1, 3)
    return np.ascontiguousarray(a, dtype=np.float32)


def kernel(L0_attn_hidden_states, L0_attn_Wr, L0_attn_Wk, L0_attn_Wv, L0_attn_Wo,
           L0_attn_time_decay, L0_attn_time_first, L0_attn_time_mix_k,
           L0_attn_time_mix_v, L0_attn_time_mix_r, L0_ffn_gate_w, L0_ffn_up_w,
           L0_ffn_down_w, L0_ffn_gate_s, L0_ffn_up_s, L0_ffn_down_s):
    x = np.asarray(L0_attn_hidden_states, dtype=np.float32)
    xt = np.ascontiguousarray(
        x.reshape(B, NCH, TC, HB, 128).transpose(0, 1, 3, 4, 2))

    wk = _tile_w(np.asarray(L0_attn_Wk).T, HB, HB)
    wv = _tile_w(np.asarray(L0_attn_Wv).T, HB, HB)
    wr = _tile_w(np.asarray(L0_attn_Wr).T, HB, HB)
    wo = _tile_w(np.asarray(L0_attn_Wo).T, HB, HB)
    wg = _tile_w(np.asarray(L0_ffn_gate_w).T, IC, HB)
    wu = _tile_w(np.asarray(L0_ffn_up_w).T, IC, HB)
    wd = _tile_w(np.asarray(L0_ffn_down_w).T, HB, IC)

    f32 = lambda a: np.ascontiguousarray(np.asarray(a, dtype=np.float32))
    td = f32(np.asarray(L0_attn_time_decay).reshape(HB, 128).T)
    tf = f32(np.asarray(L0_attn_time_first).reshape(HB, 128).T)
    tmk = f32(np.asarray(L0_attn_time_mix_k).reshape(HB, 128).T)
    tmv = f32(np.asarray(L0_attn_time_mix_v).reshape(HB, 128).T)
    tmr = f32(np.asarray(L0_attn_time_mix_r).reshape(HB, 128).T)
    gs = f32(np.asarray(L0_ffn_gate_s).reshape(IC, 128).T)
    us = f32(np.asarray(L0_ffn_up_s).reshape(IC, 128).T)
    ds = f32(np.asarray(L0_ffn_down_s).reshape(HB, 128).T)

    nc = build_nc()
    in_maps = []
    for core in range(NCORE):
        in_maps.append({
            "x_in": xt[core * BL:(core + 1) * BL],
            "wk_in": wk, "wv_in": wv, "wr_in": wr, "wo_in": wo,
            "wg_in": wg, "wu_in": wu, "wd_in": wd,
            "td_in": td, "tf_in": tf, "tmk_in": tmk, "tmv_in": tmv, "tmr_in": tmr,
            "gs_in": gs, "us_in": us, "ds_in": ds,
        })
    res = run_bass_kernel_spmd(nc, in_maps, core_ids=list(range(NCORE)))
    outs = []
    for core in range(NCORE):
        yc = res.results[core]["y_out"]
        outs.append(yc.transpose(0, 1, 4, 2, 3).reshape(BL, T, H))
    return np.ascontiguousarray(np.concatenate(outs, axis=0), dtype=np.float32)
